# version 01 (final)
# baseline (speedup 1.0000x reference)
import numpy as np
import ml_dtypes
from contextlib import ExitStack

import concourse.bass as bass
import concourse.mybir as mybir
from concourse.bass_utils import run_bass_kernel_spmd

F32 = mybir.dt.float32
BF16 = mybir.dt.bfloat16
AF = mybir.ActivationFunctionType
ALU = mybir.AluOpType
AX = mybir.AxisListType

D = 1024
N = 64
P = 64
T = 2048
NT = 16
E = 8
DFF = 4096
EPS_F32 = float(np.finfo(np.float32).eps)

DEBUG = False
SUB3 = 99


def kernel_SUB3():
    return SUB3
ENGS = ("tensor", "vector", "scalar", "sync", "gpsimd")


class Prog:
    def __init__(self, nc, ctx):
        self.nc = nc
        self.ops = {e: [] for e in ENGS}
        self.ctx = ctx
        self.sems = {}

    def sem(self, name):
        h = self.ctx.enter_context(self.nc.semaphore(name))
        self.sems[name] = [h, 0]

    def bump(self, name, k):
        s = self.sems[name]
        s[1] += k
        return s[1]

    def val(self, name):
        return self.sems[name][1]

    def emit(self, eng, fn):
        self.ops[eng].append(fn)

    def wait(self, eng, name, v):
        if v <= 0:
            return
        h = self.sems[name][0]
        self.emit(eng, lambda e, h=h, v=v: e.wait_ge(h, v))

    def instr(self, eng, fn, inc=None, inck=1):
        if inc is None:
            self.emit(eng, fn)
            return None
        h = self.sems[inc][0]
        v = self.bump(inc, inck)
        self.emit(eng, lambda e, fn=fn, h=h, k=inck: fn(e).then_inc(h, k))
        return v

    def sem_inc(self, eng, name, k=1):
        h = self.sems[name][0]
        v = self.bump(name, k)
        self.emit(eng, lambda e, h=h, k=k: e.nop().then_inc(h, k))
        return v

    def barrier(self, producers, consumers):
        for c in consumers:
            for pn in producers:
                self.wait(c, pn, self.val(pn))

    def play(self):
        ops = self.ops
        with self.nc.Block() as blk:
            @blk.tensor
            def _(e):
                for fn in ops["tensor"]:
                    fn(e)

            @blk.vector
            def _(e):
                for fn in ops["vector"]:
                    fn(e)

            @blk.scalar
            def _(e):
                for fn in ops["scalar"]:
                    fn(e)

            @blk.sync
            def _(e):
                for fn in ops["sync"]:
                    fn(e)

            @blk.gpsimd
            def _(e):
                for fn in ops["gpsimd"]:
                    fn(e)


def _finish(L):
    for name in L.sems:
        L.wait("sync", name, L.val(name))
    L.play()


def build(max_phase=99):
    nc = bass.Bass("TRN2", target_bir_lowering=False)
    ctx = ExitStack()
    sb = lambda name, shape, dt: ctx.enter_context(nc.sbuf_tensor(name, shape, dt))

    def din(name, shape, dt=F32):
        return nc.dram_tensor(name, list(shape), dt, kind="ExternalInput").ap()

    x_own = din("x_own", (T, D))
    x_prev = din("x_prev", (T, D))
    constv = din("constv", (128, 282))
    WinTr = din("WinTr", (8, 128, 8, 128))
    WoutTr = din("WoutTr", (128, 8, D))
    WrTr = din("WrTr", (128, 8, E))
    WgTr = din("WgTr", (E, 8, 128, 8, 512))
    WuTr = din("WuTr", (E, 8, 128, 8, 512))
    WdT = din("WdT", (E, DFF, D))
    A_logT = din("A_logT", (N, D))
    BT = din("BT", (N, D))
    CTd = din("CT", (N, D))
    dtb = din("dtb", (N, D))
    A_logd = din("A_logd", (128, 8, N))
    Bd = din("Bd", (128, 8, N))
    Cd = din("Cd", (128, 8, N))
    bout_row = din("bout_row", (1, D))
    esel_row = din("esel_row", (8, 8 * 128))
    ident_b = din("ident_b", (128, 128), BF16)

    out_d = nc.dram_tensor("out", [T, D], F32, kind="ExternalOutput").ap()
    hp_d = nc.dram_tensor("hp_scr", [E, 32, 4, 128, 512], BF16,
                          kind="ExternalOutput" if DEBUG else "Internal").ap()
    wd_bf = nc.dram_tensor("wd_bf", [E, DFF, D], BF16, kind="Internal").ap()
    ldaf = nc.dram_tensor("ldaf", [8, 128, 64], F32,
                          kind="ExternalOutput" if DEBUG else "Internal").ap()

    dbg = {}
    if DEBUG:
        for nm, shape, dt in [
            ("dbg_xn", [128, 8 * T], BF16), ("dbg_k", [128, 8 * 64], F32),
            ("dbg_lam", [64, D], F32), ("dbg_ybf", [128, 8 * T], BF16),
            ("dbg_comb", [8, T], F32), ("dbg_srun", [64, D], F32),
            ("dbg_lgt", [8, T], F32), ("dbg_stt", [128, 8], F32),
            ("dbg_hp0", [128, T], BF16), ("dbg_hp1", [128, T], BF16),
            ("dbg_cb", [128, T], BF16),
            ("dbg_u", [128, T], BF16),
        ]:
            dbg[nm] = nc.dram_tensor(nm, shape, dt, kind="ExternalOutput").ap()

    xn_own = sb("xn_own", [128, 8 * T], BF16)
    xn_prev = sb("xn_prev", [128, 8 * T], BF16)
    y_bf = sb("y_bf", [128, 8 * T], BF16)
    combT = sb("combT", [8, T], F32)
    k_dm = sb("k_dm", [128, 8 * 64], F32)
    lam_nm = sb("lam_nm", [64, D], F32)
    lamP_nm = sb("lamP_nm", [64, D], F32)
    dB_nm = sb("dB_nm", [64, D], F32)
    CT_nm = sb("CT_nm", [64, D], F32)
    logdA_dm = sb("logdA_dm", [128, 8 * 64], F32)
    lam_dm = sb("lam_dm", [128, 8 * 64], F32)
    S_run = sb("S_run", [64, D], F32)
    consts = sb("consts", [128, 282], F32)
    mask_ap = consts.ap()[0:64, 0:1]
    dtm_sb = consts.ap()[:, 1:9]
    bin_sb = consts.ap()[:, 9:17]
    dp_sb = consts.ap()[:, 17:25]
    iota_ap = consts.ap()[:, 25:26]
    ones_ap = consts.ap()[0:1, 26:154]
    idf_full = consts.ap()[:, 154:282]
    bout_b = sb("bout_b", [128, D], F32)
    esel_sb = sb("esel_sb", [8, 8 * 128], F32)
    idb_sb = sb("idb_sb", [128, 128], BF16)
    wr_sb = sb("wr_sb", [128, 8 * 8], F32)
    stat_sb = sb("stat_sb", [128, 128], F32)
    vjunk = sb("vjunk", [128, 16], F32)
    flushb = sb("flushb", [128, 282], F32)
    scantmp = sb("scantmp", [64, 64], F32)

    ARENA_BYTES = 66 * 1024
    arena = sb("arena", [128, ARENA_BYTES], mybir.dt.uint8)

    def carve(off, part, shape, dt):
        esz = mybir.dt.size(dt)
        n = int(np.prod(shape))
        assert off % 32 == 0 and off + n * esz <= ARENA_BYTES, (off, n * esz)
        ap = arena.ap()[0:part, off:off + n * esz].bitcast(dt)
        if len(shape) > 1:
            names = [chr(ord('a') + i) for i in range(len(shape))]
            kw = dict(zip(names, shape))
            ap = ap.rearrange(f"p ({' '.join(names)}) -> p {' '.join(names)}", **kw)
        return ap

    PSA = ctx.enter_context(nc.psum_tensor("PSA", [128, 1024], F32))
    PSB = ctx.enter_context(nc.psum_tensor("PSB", [128, 1024], F32))
    PSC = ctx.enter_context(nc.psum_tensor("PSC", [128, 1024], F32))
    PSD = ctx.enter_context(nc.psum_tensor("PSD", [128, 1024], F32))
    PSC_bf = PSC.ap().bitcast(BF16)

    L = Prog(nc, ctx)
    for s in ("sl", "sg", "st", "sv", "ss", "so", "swd", "shpw", "sf2l",
              "sgstop", "sustop", "srelu", "sh1", "shp", "swload", "swfree",
              "scbv", "stmm", "sov", "sost", "sxl"):
        L.sem(s)

    def vsp():
        L.instr("vector", lambda e: e.tensor_copy(vjunk[:, 0:8], vjunk[:, 8:16]))

    def dma(eng, out, in_, sem="sl"):
        return L.instr(eng, lambda e, o=out, i=in_: e.dma_start(out=o, in_=i),
                       inc=sem, inck=16)

    def dma_flush(eng, sem):
        return L.instr(eng, lambda e: e.dma_start(out=flushb[:], in_=constv),
                       inc=sem, inck=16)

    dma("sync", consts[:], constv)
    dma("sync", idb_sb[:], ident_b)
    boutr_sb = carve(39936, 1, (D,), F32)
    dma("sync", boutr_sb, bout_row)
    dma("sync", esel_sb[:], esel_row)
    tA = carve(0, 64, (D,), F32)
    tB = carve(4096, 64, (D,), F32)
    dma("sync", tA, A_logT)
    dma("sync", tB, dtb)
    dma("sync", dB_nm[:], BT)
    dma("sync", CT_nm[:], CTd)
    dma("sync", logdA_dm[:].rearrange("p (k n) -> p k n", k=8), A_logd)
    w_dm = carve(8192, 128, (8, 64), F32)
    c_dm = carve(10240, 128, (8, 64), F32)
    dma("sync", w_dm, Bd)
    dma("sync", c_dm, Cd)
    dma("sync", wr_sb[:].rearrange("p (k e) -> p k e", k=8), WrTr)
    for e_ in range(E):
        for piece in range(8):
            dma("gpsimd", wd_bf[e_, piece * 512:(piece + 1) * 512, :],
                WdT[e_, piece * 512:(piece + 1) * 512, :], sem="swd")

    L.barrier(["sl"], ["scalar", "vector"])
    tmp_nm = carve(12288, 64, (D,), F32)
    logdA_nm = carve(16384, 64, (D,), F32)
    L.instr("scalar", lambda e: e.activation(tmp_nm, tA, AF.Exp, scale=-1.0), inc="ss")
    L.barrier(["ss"], ["vector"])
    L.instr("vector", lambda e: e.tensor_mul(logdA_nm, tmp_nm, tB), inc="sv")
    L.barrier(["sv"], ["scalar"])
    L.instr("scalar", lambda e: e.activation(lam_nm[:], logdA_nm, AF.Exp, scale=-1.0), inc="ss")
    L.barrier(["ss"], ["vector"])
    L.instr("vector", lambda e: e.tensor_mul(dB_nm[:], dB_nm[:], tB), inc="sv")
    L.instr("vector", lambda e: e.tensor_mul(lamP_nm[:], lam_nm[:], lam_nm[:]), inc="sv")
    for _ in range(5):
        vsp()
        L.instr("vector", lambda e: e.tensor_mul(lamP_nm[:], lamP_nm[:], lamP_nm[:]), inc="sv")
    L.instr("scalar", lambda e: e.activation(lam_dm[:], logdA_dm[:], AF.Exp, scale=-1.0), inc="ss")
    L.barrier(["ss"], ["vector"])
    for k in range(8):
        L.instr("vector", lambda e, k=k: e.tensor_scalar_mul(
            logdA_dm[:, k * 64:(k + 1) * 64], lam_dm[:, k * 64:(k + 1) * 64],
            dtm_sb[:, k:k + 1]), inc="sv")
    L.barrier(["sv"], ["scalar", "sync"])
    dma("sync", ldaf.rearrange("k p n -> p k n"),
        logdA_dm[:].rearrange("p (k n) -> p k n", k=8), sem="so")
    L.instr("scalar", lambda e: e.activation(lam_dm[:], logdA_dm[:], AF.Exp, scale=-1.0), inc="ss")
    L.barrier(["ss"], ["vector"])
    for k in range(8):
        L.instr("vector", lambda e, k=k: e.tensor_scalar_mul(
            w_dm[:, k, :], w_dm[:, k, :], dtm_sb[:, k:k + 1]), inc="sv")
    vsp()
    L.instr("vector", lambda e: e.tensor_tensor(
        w_dm[:, :, :], w_dm[:, :, :], c_dm[:, :, :], ALU.mult), inc="sv")
    vsp()
    pm = carve(10240, 128, (64,), F32)
    scr_k = carve(12288, 128, (64,), F32)
    for k in range(8):
        L.instr("vector", lambda e, k=k: e.tensor_copy(
            pm, lam_dm[:, k * 64:(k + 1) * 64]), inc="sv")
        L.instr("vector", lambda e, k=k: e.reduce_sum(
            k_dm[:, k * 64:k * 64 + 1], w_dm[:, k, :], axis=AX.X), inc="sv")
        for m in range(1, 64):
            L.instr("vector", lambda e, k=k: e.tensor_tensor(
                scr_k, pm, w_dm[:, k, :], ALU.mult), inc="sv")
            if m < 63:
                L.instr("vector", lambda e, k=k: e.tensor_mul(
                    pm, pm, lam_dm[:, k * 64:(k + 1) * 64]), inc="sv")
            else:
                vsp()
            L.instr("vector", lambda e, k=k, m=m: e.reduce_sum(
                k_dm[:, k * 64 + m:k * 64 + m + 1], scr_k, axis=AX.X), inc="sv")
    for k in range(8):
        L.instr("vector", lambda e, k=k: e.tensor_tensor(
            k_dm[:, k * 64:k * 64 + 1], k_dm[:, k * 64:k * 64 + 1],
            dp_sb[:, k:k + 1], ALU.add), inc="sv")

    if max_phase < 1:
        _finish(L)
        return nc
    xt = [carve(20480, 128, (1024,), F32), carve(24576, 128, (1024,), F32)]
    sq_scr = carve(28672, 128, (1024,), F32)
    xnt = [carve(32768, 128, (1024,), BF16), carve(34816, 128, (1024,), BF16)]
    halves = [(x_own, 0), (x_prev, 1)]
    vtile_used = {}
    L.barrier(["sv", "ss"], ["sync"])

    gidx = 0
    for xsrc, hh in halves:
        xv = xsrc.rearrange("(tt p) d -> tt p d", p=128)
        c0 = hh * 64
        for tt in range(NT):
            rot = gidx % 2
            if gidx >= 2:
                L.wait("sync", "sv", vtile_used[("A", gidx - 2)])
            dma("sync", xt[rot], xv[tt])
            vload = dma_flush("sync", "sl")
            L.wait("vector", "sl", vload)
            L.instr("vector", lambda e, r=rot: e.tensor_mul(sq_scr, xt[r], xt[r]), inc="sv")
            L.instr("vector", lambda e, tt=tt, c0=c0, r=rot: e.reduce_sum(
                stat_sb[:, c0 + tt:c0 + tt + 1], xt[r], axis=AX.X), inc="sv")
            v = L.instr("vector", lambda e, tt=tt, c0=c0: e.reduce_sum(
                stat_sb[:, c0 + 16 + tt:c0 + 17 + tt], sq_scr, axis=AX.X), inc="sv")
            vtile_used[("A", gidx)] = v
            gidx += 1
    ve = carve(36864, 128, (32,), F32)
    t1 = carve(37888, 128, (32,), F32)
    S1 = [stat_sb[:, hh * 64:hh * 64 + 16] for hh in (0, 1)]
    S2 = [stat_sb[:, hh * 64 + 16:hh * 64 + 32] for hh in (0, 1)]
    MEAN = [stat_sb[:, hh * 64 + 32:hh * 64 + 48] for hh in (0, 1)]
    RSTD = [stat_sb[:, hh * 64 + 48:hh * 64 + 64] for hh in (0, 1)]
    VE = [ve[:, 0:16], ve[:, 16:32]]
    T1 = [t1[:, 0:16], t1[:, 16:32]]
    for hh in (0, 1):
        L.instr("vector", lambda e, hh=hh: e.tensor_scalar_mul(MEAN[hh], S1[hh], 1.0 / D), inc="sv")
    for hh in (0, 1):
        L.instr("vector", lambda e, hh=hh: e.tensor_scalar_mul(VE[hh], S2[hh], 1.0 / D), inc="sv")
    for hh in (0, 1):
        L.instr("vector", lambda e, hh=hh: e.tensor_mul(T1[hh], MEAN[hh], MEAN[hh]), inc="sv")
    for hh in (0, 1):
        L.instr("vector", lambda e, hh=hh: e.tensor_sub(VE[hh], VE[hh], T1[hh]), inc="sv")
    for hh in (0, 1):
        vv = L.instr("vector", lambda e, hh=hh: e.tensor_scalar_add(VE[hh], VE[hh], 1e-5), inc="sv")
    L.wait("scalar", "sv", vv)
    for hh in (0, 1):
        L.instr("scalar", lambda e, hh=hh: e.activation(T1[hh], VE[hh], AF.Ln), inc="ss")
        vs = L.instr("scalar", lambda e, hh=hh: e.activation(
            RSTD[hh], T1[hh], AF.Exp, scale=-0.5), inc="ss")
    L.wait("vector", "ss", vs)
    xnt32 = [carve(39936 + 4096, 128, (1024,), F32), carve(48128, 128, (1024,), F32)]
    xf32t = carve(52224, 128, (8, 128), F32)
    lgt = carve(20480 - 4096 + 40960, 8, (T,), F32)
    lgt = carve(57344, 8, (T,), F32)
    gidx = 0
    ti = 0
    evac_v = {}
    for (xsrc, hh), xdst in zip(halves, (xn_own, xn_prev)):
        xv = xsrc.rearrange("(tt p) d -> tt p d", p=128)
        c0 = hh * 64
        for tt in range(NT):
            rot = gidx % 2
            if gidx >= 2:
                L.wait("sync", "sv", vtile_used[("B", gidx - 2)])
            dma("sync", xt[rot], xv[tt])
            vload = dma_flush("sync", "sl")
            L.wait("vector", "sl", vload)
            if gidx >= 2:
                L.wait("vector", "st", evac_v[("tp", gidx - 2)])
            v = L.instr("vector", lambda e, r=rot, tt=tt, c0=c0: e.tensor_scalar(
                xnt[r], xt[r], stat_sb[:, c0 + 32 + tt:c0 + 33 + tt],
                stat_sb[:, c0 + 48 + tt:c0 + 49 + tt],
                ALU.subtract, ALU.mult), inc="sv")
            if hh == 0:
                v = L.instr("vector", lambda e, r=rot, tt=tt, c0=c0: e.tensor_scalar(
                    xnt32[r], xt[r], stat_sb[:, c0 + 32 + tt:c0 + 33 + tt],
                    stat_sb[:, c0 + 48 + tt:c0 + 49 + tt],
                    ALU.subtract, ALU.mult), inc="sv")
            vtile_used[("B", gidx)] = v
            L.wait("tensor", "sv", v)
            for k in range(8):
                slot = ti % 2
                pv = PSC_bf[:, slot * 1024: slot * 1024 + 128]
                if ti >= 2:
                    L.wait("tensor", "ss", evac_v[ti - 2])
                vt = L.instr("tensor", lambda e, r=rot, k=k, pv=pv: e.transpose(
                    pv, xnt[r][:, k * 128:(k + 1) * 128], idb_sb[:]), inc="st")
                L.wait("scalar", "st", vt)
                evac_v[ti] = L.instr("scalar", lambda e, pv=pv, k=k, tt=tt, xdst=xdst: e.copy(
                    xdst[:, k * T + tt * 128:k * T + (tt + 1) * 128], pv), inc="ss")
                ti += 1
                if hh == 0:
                    slot2 = ti % 2
                    pv2 = PSC[:, slot2 * 512: slot2 * 512 + 128]
                    if ti >= 2:
                        L.wait("tensor", "ss", evac_v[ti - 2])
                    vt2 = L.instr("tensor", lambda e, r=rot, k=k, pv2=pv2: e.transpose(
                        pv2, xnt32[r][:, k * 128:(k + 1) * 128], idf_full), inc="st")
                    L.wait("scalar", "st", vt2)
                    evac_v[ti] = L.instr("scalar", lambda e, pv2=pv2, k=k: e.copy(
                        xf32t[:, k, :], pv2), inc="ss")
                    ti += 1
            evac_v[("tp", gidx)] = L.val("st")
            if hh == 0:
                L.wait("tensor", "ss", L.val("ss"))
                L.wait("tensor", "sv", L.val("sv"))
                pl = PSD[0:8, (tt % 2) * 512:(tt % 2) * 512 + 128]
                for k in range(8):
                    L.instr("tensor", lambda e, k=k, pl=pl: e.matmul(
                        pl, wr_sb[:, k * 8:(k + 1) * 8], xf32t[:, k, :],
                        start=(k == 0), stop=(k == 7)), inc="st")
                L.wait("vector", "st", L.val("st"))
                L.instr("vector", lambda e, tt=tt, pl=pl: e.tensor_copy(
                    lgt[:, tt * 128:(tt + 1) * 128], pl), inc="sv")
            gidx += 1
    L.barrier(["ss", "sv", "st", "sl", "sg"],
              ["tensor", "vector", "scalar", "sync"])

    if max_phase < 2:
        _finish(L)
        return nc
    stt = carve(28672, 128, (16, 8), F32)
    mk = carve(29696, 128, (16, 8), F32)
    srw = carve(30720, 128, (16, 8), F32)
    tpev = {}
    for tt in range(NT):
        slot = tt % 2
        pv = PSC[:, slot * 512: slot * 512 + 8]
        if tt >= 2:
            L.wait("tensor", "sv", tpev[tt - 2])
        L.instr("tensor", lambda e, tt=tt, pv=pv: e.matmul(
            pv, lgt[:, tt * 128:(tt + 1) * 128], idf_full[0:8, 0:8],
            start=True, stop=True), inc="st")
        L.wait("vector", "st", L.val("st"))
        tpev[tt] = L.instr("vector", lambda e, tt=tt, pv=pv: e.tensor_copy(
            stt[:, tt, :], pv), inc="sv")
    for tt in range(NT):
        L.instr("vector", lambda e, tt=tt: e.reduce_max(
            srw[:, tt, 0:1], stt[:, tt, :], axis=AX.X), inc="sv")
    for tt in range(NT):
        vv = L.instr("vector", lambda e, tt=tt: e.tensor_scalar_mul(
            srw[:, tt, 1:2], srw[:, tt, 0:1], -1.0), inc="sv")
    L.wait("scalar", "sv", vv)
    for tt in range(NT):
        vs = L.instr("scalar", lambda e, tt=tt: e.activation(
            stt[:, tt, :], stt[:, tt, :], AF.Exp, bias=srw[:, tt, 1:2], scale=1.0), inc="ss")
    L.wait("vector", "ss", vs)
    for tt in range(NT):
        vv = L.instr("vector", lambda e, tt=tt: e.reduce_sum(
            srw[:, tt, 2:3], stt[:, tt, :], axis=AX.X), inc="sv")
    L.wait("scalar", "sv", vv)
    for tt in range(NT):
        L.instr("scalar", lambda e, tt=tt: e.activation(
            srw[:, tt, 3:4], srw[:, tt, 2:3], AF.Ln), inc="ss")
        vs = L.instr("scalar", lambda e, tt=tt: e.activation(
            srw[:, tt, 3:4], srw[:, tt, 3:4], AF.Exp, scale=-1.0), inc="ss")
    L.wait("vector", "ss", vs)
    for tt in range(NT):
        L.instr("vector", lambda e, tt=tt: e.tensor_scalar_mul(
            stt[:, tt, :], stt[:, tt, :], srw[:, tt, 3:4]), inc="sv")
    for tt in range(NT):
        L.instr("vector", lambda e, tt=tt: e.reduce_max(
            srw[:, tt, 4:5], stt[:, tt, :], axis=AX.X), inc="sv")
    for tt in range(NT):
        L.instr("vector", lambda e, tt=tt: e.tensor_single_scalar(
            mk[:, tt, :], stt[:, tt, :], srw[:, tt, 4:5], ALU.is_equal), inc="sv")
    for tt in range(NT):
        L.instr("vector", lambda e, tt=tt: e.scalar_tensor_tensor(
            mk[:, tt, :], mk[:, tt, :], -2.0, stt[:, tt, :], ALU.mult, ALU.add), inc="sv")
    for tt in range(NT):
        L.instr("vector", lambda e, tt=tt: e.reduce_max(
            srw[:, tt, 5:6], mk[:, tt, :], axis=AX.X), inc="sv")
    for tt in range(NT):
        L.instr("vector", lambda e, tt=tt: e.tensor_tensor(
            srw[:, tt, 6:7], srw[:, tt, 4:5], srw[:, tt, 5:6], ALU.add), inc="sv")
    for tt in range(NT):
        vv = L.instr("vector", lambda e, tt=tt: e.tensor_scalar_add(
            srw[:, tt, 6:7], srw[:, tt, 6:7], EPS_F32), inc="sv")
    L.wait("scalar", "sv", vv)
    for tt in range(NT):
        L.instr("scalar", lambda e, tt=tt: e.activation(
            srw[:, tt, 7:8], srw[:, tt, 6:7], AF.Ln), inc="ss")
        vs = L.instr("scalar", lambda e, tt=tt: e.activation(
            srw[:, tt, 7:8], srw[:, tt, 7:8], AF.Exp, scale=-1.0), inc="ss")
    for tt in range(NT):
        L.instr("vector", lambda e, tt=tt: e.tensor_single_scalar(
            mk[:, tt, :], stt[:, tt, :], srw[:, tt, 5:6], ALU.is_ge), inc="sv")
    L.wait("vector", "ss", vs)
    for tt in range(NT):
        L.instr("vector", lambda e, tt=tt: e.tensor_mul(
            stt[:, tt, :], stt[:, tt, :], mk[:, tt, :]), inc="sv")
    for tt in range(NT):
        vv = L.instr("vector", lambda e, tt=tt: e.tensor_scalar_mul(
            stt[:, tt, :], stt[:, tt, :], srw[:, tt, 7:8]), inc="sv")
    L.wait("tensor", "sv", vv)
    cbev = {}
    for tt in range(NT):
        slot = tt % 2
        pv2 = PSC[0:8, 256 + slot * 512: 256 + slot * 512 + 128]
        if tt >= 2:
            L.wait("tensor", "sv", cbev[tt - 2])
        L.instr("tensor", lambda e, tt=tt, pv2=pv2: e.matmul(
            pv2, stt[:, tt, :], idf_full[:, 0:128], start=True, stop=True), inc="st")
        L.wait("vector", "st", L.val("st"))
        cbev[tt] = L.instr("vector", lambda e, tt=tt, pv2=pv2: e.tensor_copy(
            combT[:, tt * 128:(tt + 1) * 128], pv2), inc="sv")
    L.wait("tensor", "sv", L.val("sv"))
    for cpart in range(2):
        L.instr("tensor", lambda e, c=cpart: e.matmul(
            PSA[:, c * 512:(c + 1) * 512], ones_ap,
            boutr_sb[0:1, c * 512:(c + 1) * 512], start=True, stop=True), inc="st")
    L.wait("vector", "st", L.val("st"))
    L.instr("vector", lambda e: e.tensor_copy(bout_b[:, 0:512], PSA[:, 0:512]), inc="sv")
    L.instr("vector", lambda e: e.tensor_copy(bout_b[:, 512:1024], PSA[:, 512:1024]), inc="sv")
    L.barrier(["sv", "ss", "st"], ["tensor", "vector", "scalar", "sync", "gpsimd"])

    if DEBUG:
        L.wait("sync", "sv", L.val("sv"))
        dma("sync", dbg["dbg_xn"], xn_own[:], sem="so")
        dma("sync", dbg["dbg_k"], k_dm[:], sem="so")
        dma("sync", dbg["dbg_lam"], lam_nm[:], sem="so")
        dma("sync", dbg["dbg_comb"], combT[:], sem="so")
        dma("sync", dbg["dbg_lgt"], lgt, sem="so")

    if max_phase < 3:
        _finish(L)
        return nc
    u_blk = carve(0, 128, (32, 64), BF16)
    u_flat = carve(0, 128, (2048,), BF16)
    up_blk = carve(4096, 128, (2048,), BF16)
    uT = carve(8192, 128, (128, 32), BF16)
    winf = carve(16384, 128, (8, 128), F32)
    wing = carve(20480, 128, (8, 128), BF16)
    Ppow = carve(16384, 128, (64, 64), BF16)
    Zt = carve(24576, 64, (64, 64), F32)
    CLt = carve(24576, 64, (64, 64), F32)
    flat = carve(24576, 1, (4096,), F32)
    SBst = carve(40960, 64, (64, 32), F32)
    yTs = carve(49152, 64, (128, 32), F32)
    y_blk = carve(8192, 128, (32, 64), F32)
    y4 = carve(8192, 128, (16, 2, 64), F32)
    u4 = carve(0, 128, (16, 2, 64), BF16)

    for dblk in range(8):
        dma("sync", winf, WinTr[dblk])
        vwl = dma_flush("sync", "sl")
        L.wait("scalar", "sl", vwl)
        vws = L.instr("scalar", lambda e: e.copy(wing[:, :, :], winf[:, :, :]), inc="ss")
        L.wait("tensor", "ss", vws)
        ip = 0
        ipev = {}
        for xnsrc, udst in ((xn_own, u_flat), (xn_prev, up_blk)):
            for ts in range(4):
                slot = ip % 2
                pv = PSA[:, slot * 512:(slot + 1) * 512]
                if ip >= 2:
                    L.wait("tensor", "ss", ipev[ip - 2])
                for dk in range(8):
                    L.instr("tensor", lambda e, pv=pv, dk=dk, ts=ts, xnsrc=xnsrc: e.matmul(
                        pv, wing[:, dk, :],
                        xnsrc[:, dk * T + ts * 512:dk * T + (ts + 1) * 512],
                        start=(dk == 0), stop=(dk == 7)), inc="st")
                L.wait("scalar", "st", L.val("st"))
                ipev[ip] = L.instr("scalar", lambda e, pv=pv, ts=ts, udst=udst, dblk=dblk: e.activation(
                    udst[:, ts * 512:(ts + 1) * 512], pv, AF.Identity,
                    bias=bin_sb[:, dblk:dblk + 1], scale=1.0), inc="ss")
                ip += 1
        L.barrier(["ss"], ["tensor"])
        if DEBUG and dblk == 0:
            L.wait("sync", "ss", L.val("ss"))
            dma("sync", dbg["dbg_u"], u_flat, sem="so")
        if kernel_SUB3() < 2:
            continue
        tpi = 0
        tpev = {}
        for half_i, usrc in ((0, up_blk), (1, u_flat)):
            for tt in range(NT):
                slot = tpi % 2
                pv = PSC_bf[:, slot * 1024: slot * 1024 + 128]
                if tpi >= 2:
                    L.wait("tensor", "sv", tpev[tpi - 2])
                vt = L.instr("tensor", lambda e, usrc=usrc, tt=tt, pv=pv: e.transpose(
                    pv, usrc[:, tt * 128:(tt + 1) * 128], idb_sb[:]), inc="st")
                L.wait("vector", "st", vt)
                gi = half_i * 16 + tt
                tpev[tpi] = L.instr("vector", lambda e, pv=pv, gi=gi: e.tensor_copy(
                    uT[:, :, gi], pv), inc="sv")
                tpi += 1
        L.barrier(["sv"], ["tensor", "sync"])
        if kernel_SUB3() < 3:
            continue
        for h2 in range(2):
            L.wait("sync", "st", L.val("st"))
            L.wait("sync", "sv", L.val("sv"))
            L.wait("sync", "so", 16)
            dma("sync", flat[0:1, :],
                ldaf[dblk, 64 * h2:64 * (h2 + 1), :].rearrange("p n -> (p n)").unsqueeze(0))
            vfl = dma_flush("sync", "sl")
            L.wait("tensor", "sl", vfl)
            ppev = {}
            for cchunk in range(8):
                slot = cchunk % 2
                pv = PSD[:, slot * 512:(slot + 1) * 512]
                if cchunk >= 2:
                    L.wait("tensor", "ss", ppev[cchunk - 2])
                L.instr("tensor", lambda e, c=cchunk, pv=pv: e.matmul(
                    pv, ones_ap, flat[0:1, c * 512:(c + 1) * 512],
                    start=True, stop=True), inc="st")
                L.wait("scalar", "st", L.val("st"))
                ppev[cchunk] = L.instr("scalar", lambda e, pv=pv, c=cchunk: e.activation(
                    Ppow[:].rearrange("p a b -> p (a b)")[:, c * 512:(c + 1) * 512],
                    pv, AF.Exp, scale=iota_ap), inc="ss")
            L.barrier(["ss"], ["tensor"])
            if kernel_SUB3() < 4:
                continue
            zev = {}
            for dl in range(64):
                slot = dl % 2
                pz = PSD[0:64, slot * 512: slot * 512 + 64]
                if dl >= 2:
                    L.wait("tensor", "sv", zev[dl - 2])
                dle = h2 * 64 + dl
                for q in range(2):
                    L.instr("tensor", lambda e, dl=dl, dle=dle, q=q, pz=pz: e.matmul(
                        pz[:, q * 32:(q + 1) * 32],
                        Ppow[64 * q:64 * (q + 1), dl, :],
                        uT[64 * q:64 * (q + 1), dle, :],
                        start=True, stop=True), inc="st")
                L.wait("vector", "st", L.val("st"))
                zin = pz.rearrange("p (q j) -> p q j", q=2)
                zout = Zt[:, dl, :].rearrange("p (j q) -> p q j", q=2)
                zev[dl] = L.instr("vector", lambda e, zout=zout, zin=zin: e.tensor_copy(
                    zout, zin), inc="sv")
            if kernel_SUB3() < 5:
                continue
            dsl = dB_nm[:, dblk * 128 + h2 * 64: dblk * 128 + (h2 + 1) * 64]
            vsp()
            L.instr("vector", lambda e, dsl=dsl: e.tensor_tensor(
                Zt[:, :, :], Zt[:, :, :],
                dsl.unsqueeze(2).broadcast_to([64, 64, 64]), ALU.mult), inc="sv")
            vsp()
            base = dblk * 128 + h2 * 64
            SR = [S_run[:, base:base + 32], S_run[:, base + 32:base + 64]]
            LS = [lamP_nm[:, base:base + 32], lamP_nm[:, base + 32:base + 64]]
            ZL = [Zt[:, 0:32, :], Zt[:, 32:64, :]]
            SBL = [SBst[:, 0:32, :], SBst[:, 32:64, :]]
            TM = [scantmp[:, 0:32], scantmp[:, 32:64]]
            for ln in (0, 1):
                L.instr("vector", lambda e, ln=ln, ZL=ZL, SR=SR: e.tensor_copy(
                    SR[ln], ZL[ln][:, :, 0]), inc="sv")
            for pc in range(1, 64):
                if pc == 32:
                    for ln in (0, 1):
                        L.instr("vector", lambda e, ln=ln, SR=SR: e.tensor_scalar_mul(
                            SR[ln], SR[ln], mask_ap), inc="sv")
                if pc >= 32:
                    for ln in (0, 1):
                        L.instr("vector", lambda e, ln=ln, pc=pc, SBL=SBL, SR=SR: e.tensor_copy(
                            SBL[ln][:, :, pc - 32], SR[ln]), inc="sv")
                for ln in (0, 1):
                    L.instr("vector", lambda e, ln=ln, TM=TM, SR=SR, LS=LS: e.tensor_mul(
                        TM[ln], SR[ln], LS[ln]), inc="sv")
                for ln in (0, 1):
                    L.instr("vector", lambda e, ln=ln, pc=pc, SR=SR, TM=TM, ZL=ZL: e.tensor_tensor(
                        SR[ln], TM[ln], ZL[ln][:, :, pc], ALU.add), inc="sv")
            if kernel_SUB3() < 6:
                continue
            csl = CT_nm[:, base:base + 64]
            lms = lam_nm[:, base:base + 64]
            CLL = [CLt[:, 0:32, :], CLt[:, 32:64, :]]
            CS = [csl[:, 0:32], csl[:, 32:64]]
            LM = [lms[:, 0:32], lms[:, 32:64]]
            for ln in (0, 1):
                L.instr("vector", lambda e, ln=ln, CLL=CLL, CS=CS, LM=LM: e.tensor_tensor(
                    CLL[ln][:, :, 0], CS[ln], LM[ln], ALU.mult), inc="sv")
            for dlt in range(1, 64):
                for ln in (0, 1):
                    L.instr("vector", lambda e, ln=ln, dlt=dlt, CLL=CLL, LM=LM: e.tensor_tensor(
                        CLL[ln][:, :, dlt], CLL[ln][:, :, dlt - 1], LM[ln],
                        ALU.mult), inc="sv")
            L.barrier(["sv"], ["tensor"])
            soev = {}
            for dl in range(64):
                slot = dl % 2
                pz = PSD[0:64, slot * 512: slot * 512 + 32]
                if dl >= 2:
                    L.wait("tensor", "sv", soev[dl - 2])
                dle = h2 * 64 + dl
                L.instr("tensor", lambda e, dl=dl, pz=pz: e.matmul(
                    pz, CLt[:, dl, :], SBst[:, dl, :], start=True, stop=True), inc="st")
                L.wait("vector", "st", L.val("st"))
                soev[dl] = L.instr("vector", lambda e, dle=dle, pz=pz: e.tensor_copy(
                    yTs[:, dle, :], pz), inc="sv")
            L.barrier(["sv"], ["tensor", "sync"])
        if kernel_SUB3() < 7:
            continue
        L.instr("vector", lambda e: e.memset(y_blk[:, :, :], 0.0), inc="sv")
        vsp()
        L.barrier(["sv"], ["tensor"])
        asev = {}
        for pc in range(32):
            slot = pc % 2
            pv = PSC[:, slot * 512: slot * 512 + 64]
            if pc >= 2:
                L.wait("tensor", "sv", asev[pc - 2])
            vt = L.instr("tensor", lambda e, pc=pc, pv=pv: e.transpose(
                pv, yTs[:, :, pc], idf_full[0:64, 0:64]), inc="st")
            L.wait("vector", "st", vt)
            asev[pc] = L.instr("vector", lambda e, pc=pc, pv=pv: e.tensor_tensor(
                y_blk[:, pc, :], y_blk[:, pc, :], pv, ALU.add), inc="sv")
        L.barrier(["sv"], ["tensor", "sync"])
        if kernel_SUB3() < 8:
            continue
        for m in range(64):
            kap = k_dm[:, dblk * 64 + m: dblk * 64 + m + 1]
            for par in (0, 1):
                L.instr("vector", lambda e, kap=kap, m=m, par=par: e.scalar_tensor_tensor(
                    y4[:, :, par, m:64], u4[:, :, par, 0:64 - m], kap,
                    y4[:, :, par, m:64], ALU.mult, ALU.add), inc="sv")
        vsp()
        L.instr("vector", lambda e, dblk=dblk: e.tensor_copy(
            y_bf[:, dblk * T:(dblk + 1) * T],
            y_blk[:].rearrange("p a b -> p (a b)")), inc="sv")
        L.barrier(["sv"], ["tensor", "scalar", "sync", "gpsimd"])

    if DEBUG:
        L.wait("sync", "sv", L.val("sv"))
        dma("sync", dbg["dbg_ybf"], y_bf[:], sem="so")
        dma("sync", dbg["dbg_srun"], S_run[:], sem="so")

    if max_phase < 4:
        _finish(L)
        return nc
    NWB = 3
    wg_bf = [carve(8192 * j, 128, (8, 512), BF16) for j in range(NWB)]
    wu_bf = [carve(24576 + 8192 * j, 128, (8, 512), BF16) for j in range(NWB)]
    rg_b = [carve(49152, 128, (512,), F32), carve(51200, 128, (512,), F32)]
    h1_b = [carve(53248, 128, (512,), F32), carve(55296, 128, (512,), F32)]
    hp_b = [carve(57344, 128, (512,), BF16), carve(58368, 128, (512,), BF16)]
    cb_e = [xn_prev[:, 0:T], xn_prev[:, T:2 * T]]

    idx = 0
    wfree_vals = {}
    cb_done = {}
    for e_ in range(E):
        pe = e_ % 2
        cbev = {}
        for cpart in range(4):
            cslot = cpart % 2
            pv = PSC[:, cslot * 512:(cslot + 1) * 512]
            if cpart >= 2:
                L.wait("tensor", "sv", cbev[cpart - 2])
            elif e_ > 0:
                L.wait("tensor", "sv", cb_done[e_ - 1])
            L.instr("tensor", lambda e, e_=e_, c=cpart, pv=pv: e.matmul(
                pv, esel_sb[0:8, e_ * 128:(e_ + 1) * 128],
                combT[0:8, c * 512:(c + 1) * 512],
                start=True, stop=True), inc="st")
            L.wait("vector", "st", L.val("st"))
            cbev[cpart] = L.instr("vector", lambda e, pe=pe, c=cpart, pv=pv: e.tensor_copy(
                cb_e[pe][:, c * 512:(c + 1) * 512], pv),
                inc="sv")
        cb_done[e_] = L.val("sv")
        for fg in range(8):
            sidx = e_ * 8 + fg
            sslot = sidx % NWB
            if sidx >= NWB:
                L.wait("gpsimd", "sustop", wfree_vals[sidx - NWB])
            dma("gpsimd", wg_bf[sslot], WgTr[e_, fg], sem="swload")
            dma("gpsimd", wu_bf[sslot], WuTr[e_, fg], sem="swload")
            vw = dma_flush("gpsimd", "swload")
            L.wait("tensor", "swload", vw)
            for ftl in range(4):
                for ts in range(4):
                    par = idx % 2
                    gp = PSA[:, par * 512:(par + 1) * 512]
                    up = PSB[:, par * 512:(par + 1) * 512]
                    if idx >= 2:
                        L.wait("tensor", "srelu", idx - 1)
                        L.wait("tensor", "sh1", idx - 1)
                    for dk in range(8):
                        L.instr("tensor", lambda e, gp=gp, s=sslot, dk=dk, ftl=ftl, ts=ts: e.matmul(
                            gp, wg_bf[s][:, dk, ftl * 128:(ftl + 1) * 128],
                            xn_own[:, dk * T + ts * 512:dk * T + (ts + 1) * 512],
                            start=(dk == 0), stop=(dk == 7)),
                            inc="sgstop" if dk == 7 else None)
                    for dk in range(8):
                        L.instr("tensor", lambda e, up=up, s=sslot, dk=dk, ftl=ftl, ts=ts: e.matmul(
                            up, wu_bf[s][:, dk, ftl * 128:(ftl + 1) * 128],
                            xn_own[:, dk * T + ts * 512:dk * T + (ts + 1) * 512],
                            start=(dk == 0), stop=(dk == 7)),
                            inc="sustop" if dk == 7 else None)
                    if ftl == 3 and ts == 3:
                        wfree_vals[sidx] = L.val("sustop")
                    L.wait("scalar", "sgstop", idx + 1)
                    if idx >= 2:
                        L.wait("scalar", "sh1", idx - 1)
                    L.instr("scalar", lambda e, gp=gp, par=par: e.activation(
                        rg_b[par], gp, AF.Relu), inc="srelu")
                    L.wait("vector", "srelu", idx + 1)
                    L.wait("vector", "sustop", idx + 1)
                    L.instr("vector", lambda e, up=up, par=par: e.tensor_tensor(
                        h1_b[par], rg_b[par], up, ALU.mult), inc="sh1")
                    if idx >= 2:
                        L.wait("vector", "shpw", 16 * (idx - 1))
                    vsp()
                    L.instr("vector", lambda e, par=par, pe=pe, ts=ts: e.tensor_tensor(
                        hp_b[par], h1_b[par], cb_e[pe][:, ts * 512:(ts + 1) * 512],
                        ALU.mult), inc="shp")
                    L.wait("sync", "shp", idx + 1)
                    ft = fg * 4 + ftl
                    dma("sync", hp_d[e_, ft, ts], hp_b[par], sem="shpw")
                    idx += 1
    L.barrier(["shpw", "sv", "ss", "st", "sgstop", "sustop", "srelu", "sh1",
               "shp", "swd", "sg"], ["tensor", "vector", "scalar", "sync"])

    if max_phase < 5:
        _finish(L)
        return nc
    if DEBUG:
        L.wait("sync", "shpw", L.val("shpw"))
        L.wait("sync", "sv", L.val("sv"))
        for g4 in range(4):
            dma("sync", dbg["dbg_hp0"][:, g4 * 512:(g4 + 1) * 512], hp_d[0, 3, g4], sem="so")
            dma("sync", dbg["dbg_hp1"][:, g4 * 512:(g4 + 1) * 512], hp_d[0, 17, g4], sem="so")
        dma("sync", dbg["dbg_cb"], cb_e[0], sem="so")

    woutf = carve(0, 128, (8, 1024), F32)
    NF2B = 3
    hp_t = [carve(1024 * j, 128, (512,), BF16) for j in range(NF2B)]
    wd_t = [carve(14336 + 2048 * j, 128, (1024,), BF16) for j in range(NF2B)]
    o_sb = [carve(6144, 128, (1024,), F32), carve(10240, 128, (1024,), F32)]
    woutg = carve(32768, 128, (8, 1024), BF16)
    xtb = carve(49152, 128, (4, 1024), F32)
    dma("sync", woutf, WoutTr)
    vwo = dma_flush("sync", "sl")
    L.wait("scalar", "sl", vwo)
    vws = L.instr("scalar", lambda e: e.copy(woutg[:, :, :], woutf[:, :, :]), inc="ss")
    L.wait("tensor", "ss", vws)
    L.wait("sync", "swd", L.val("swd"))

    PS = [PSA, PSB, PSC, PSD]
    xov = x_own.rearrange("(g tt p) d -> g p tt d", p=128, tt=4)
    oov = out_d.rearrange("(g tt p) d -> g p tt d", p=128, tt=4)
    f2i = 0
    for ttg in range(4):
        if ttg > 0:
            L.wait("sync", "sov", L.val("sov"))
        for tl4 in range(4):
            dma("sync", xtb[:, tl4, :], xov[ttg][:, tl4, :], sem="sxl")
        vxl = dma_flush("sync", "sxl")
        for e_ in range(E):
            for ft in range(32):
                par = f2i % NF2B
                if f2i >= NF2B:
                    L.wait("sync", "stmm", f2i - NF2B + 1)
                dma("sync", hp_t[par], hp_d[e_, ft, ttg], sem="sf2l")
                dma("sync", wd_t[par], wd_bf[e_, ft * 128:(ft + 1) * 128, :],
                    sem="sf2l")
                vf2 = dma_flush("sync", "sf2l")
                L.wait("tensor", "sf2l", vf2)
                first = (e_ == 0 and ft == 0)
                for tl in range(4):
                    for cc in range(2):
                        last = (tl == 3 and cc == 1)
                        L.instr("tensor", lambda e, par=par, tl=tl, cc=cc, first=first: e.matmul(
                            PS[tl][:, cc * 512:(cc + 1) * 512],
                            hp_t[par][:, tl * 128:(tl + 1) * 128],
                            wd_t[par][:, cc * 512:(cc + 1) * 512],
                            start=first, stop=False),
                            inc="stmm" if last else None)
                f2i += 1
        for dk in range(8):
            for tl in range(4):
                for cc in range(2):
                    last = (dk == 7 and tl == 3 and cc == 1)
                    L.instr("tensor", lambda e, dk=dk, tl=tl, cc=cc, ttg=ttg: e.matmul(
                        PS[tl][:, cc * 512:(cc + 1) * 512],
                        y_bf[:, dk * T + ttg * 512 + tl * 128:
                             dk * T + ttg * 512 + (tl + 1) * 128],
                        woutg[:, dk, cc * 512:(cc + 1) * 512],
                        start=False, stop=(dk == 7)),
                        inc="st" if last else None)
        L.wait("vector", "st", L.val("st"))
        L.wait("vector", "sxl", vxl)
        L.instr("vector", lambda e: e.tensor_tensor(
            xtb[:, :, :], xtb[:, :, :],
            bout_b[:].unsqueeze(1).broadcast_to([128, 4, 1024]), ALU.add), inc="sv")
        vsp()
        for tl in range(4):
            po = tl % 2
            oidx = ttg * 4 + tl
            if oidx >= 2:
                L.wait("vector", "sost", 16 * (oidx - 1))
            v = L.instr("vector", lambda e, tl=tl, po=po: e.tensor_tensor(
                o_sb[po], PS[tl][:, :], xtb[:, tl, :], ALU.add), inc="sov")
            L.wait("sync", "sov", v)
            dma("sync", oov[ttg][:, tl, :], o_sb[po], sem="sost")
        L.wait("tensor", "sov", L.val("sov"))

    L.wait("sync", "sost", L.val("sost"))
    if DEBUG:
        L.wait("sync", "so", L.val("so"))
    _finish(L)
    return nc


def _prep(inputs):
    f32 = np.float32
    x = np.ascontiguousarray(inputs["L0_attn_x"], dtype=f32)
    Win = np.asarray(inputs["L0_attn_Win"], f32)
    bin_ = np.asarray(inputs["L0_attn_bin_"], f32)
    A_log = np.asarray(inputs["L0_attn_A_log"], f32)
    B = np.asarray(inputs["L0_attn_B"], f32)
    C = np.asarray(inputs["L0_attn_C"], f32)
    Dp = np.asarray(inputs["L0_attn_Dp"], f32)
    dt = np.asarray(inputs["L0_attn_dt"], f32)
    Wout = np.asarray(inputs["L0_attn_Wout"], f32)
    bout = np.asarray(inputs["L0_attn_bout"], f32)
    Wr = np.asarray(inputs["L0_ffn_Wr"], f32)
    Wg = np.asarray(inputs["L0_ffn_Wg"], f32)
    Wu = np.asarray(inputs["L0_ffn_Wu"], f32)
    Wd = np.asarray(inputs["L0_ffn_Wd"], f32)

    col = lambda v: np.ascontiguousarray(v.reshape(8, 128).T)
    dmaj = lambda v: np.ascontiguousarray(v.reshape(8, 128, N).transpose(1, 0, 2))
    WgT = Wg.transpose(0, 2, 1)
    WuT = Wu.transpose(0, 2, 1)
    shared = {
        "WinTr": np.ascontiguousarray(
            Win.T.reshape(8, 128, 8, 128).transpose(2, 1, 0, 3)),
        "WoutTr": np.ascontiguousarray(
            Wout.T.reshape(8, 128, D).transpose(1, 0, 2)),
        "WrTr": np.ascontiguousarray(
            Wr.T.reshape(8, 128, E).transpose(1, 0, 2)),
        "WgTr": np.ascontiguousarray(
            WgT.reshape(E, 8, 128, 8, 512).transpose(0, 3, 2, 1, 4)),
        "WuTr": np.ascontiguousarray(
            WuT.reshape(E, 8, 128, 8, 512).transpose(0, 3, 2, 1, 4)),
        "WdT": np.ascontiguousarray(Wd.transpose(0, 2, 1)),
        "A_logT": np.ascontiguousarray(A_log.T),
        "BT": np.ascontiguousarray(B.T),
        "CT": np.ascontiguousarray(C.T),
        "dtb": np.ascontiguousarray(np.broadcast_to(dt[None, :], (N, D))),
        "A_logd": dmaj(A_log),
        "Bd": dmaj(B),
        "Cd": dmaj(C),
        "bout_row": np.ascontiguousarray(bout.reshape(1, D)),
        "esel_row": np.ascontiguousarray(
            np.repeat(np.eye(8, dtype=f32), 128, axis=1)),
        "ident_b": np.eye(128).astype(ml_dtypes.bfloat16),
    }
    constv0 = np.zeros((128, 282), f32)
    constv0[:, 1:9] = col(dt)
    constv0[:, 9:17] = col(bin_)
    constv0[:, 17:25] = col(Dp)
    constv0[:, 25] = (np.arange(128) % 64) - 63.0
    constv0[0, 26:154] = 1.0
    constv0[:, 154:282] = np.eye(128, dtype=f32)
    percore = []
    for c in range(8):
        b, h = c // 2, c % 2
        cv = constv0.copy()
        cv[0:64, 0] = float(h)
        percore.append({
            "x_own": np.ascontiguousarray(x[b, h * T:(h + 1) * T]),
            "x_prev": np.ascontiguousarray(x[b, 0:T]),
            "constv": cv,
        })
    return shared, percore


MAX_PHASE = 99


def kernel(**inputs):
    nc = build(MAX_PHASE)
    shared, percore = _prep(inputs)
    in_maps = [{**shared, **pc} for pc in percore]
    res = run_bass_kernel_spmd(nc, in_maps, core_ids=list(range(8)))
    out = np.empty((4, 4096, D), np.float32)
    for c in range(8):
        b, h = c // 2, c % 2
        out[b, h * T:(h + 1) * T] = res.results[c]["out"]
    if DEBUG:
        kernel._dbg = res.results
    return out
